# version 01
# baseline (speedup 1.0000x reference)
import numpy as np
import ml_dtypes
import concourse.bass as bass
import concourse.mybir as mybir
from concourse.bass_utils import run_bass_kernel_spmd
from contextlib import ExitStack

F32 = mybir.dt.float32
BF16 = mybir.dt.bfloat16
I8 = mybir.dt.int8
AF = mybir.ActivationFunctionType
ALU = mybir.AluOpType
AX = mybir.AxisListType
BF16NP = ml_dtypes.bfloat16

NCORES = 8
TLOC = 2048
HALO = 1024
TEXT = TLOC + HALO
D = 2048
NT = D // 128
NH, NKV, HD = 16, 4, 128
WIN = 1024
KSPAN = WIN + 128
FFH = 8192
TC = 512
NCH = TEXT // TC
NCO = TLOC // TC
SQH = float(np.sqrt(HD))
DBG = []
import os
PHASE_LIMIT = int(os.environ.get("KPHASE", "9"))
SUB_A1A = int(os.environ.get("KSUBA", "9"))
NCH_LIM = int(os.environ.get("KNCH", str(TEXT // TC)))
K1OUT = int(os.environ.get("K1OUT", "0"))
KDUMPA = int(os.environ.get("KDUMPA", "0"))
KB2 = int(os.environ.get("KB2", "1"))
KA2S2 = int(os.environ.get("KA2S2", "1"))
KNIT = int(os.environ.get("KNIT", "64"))
KC1W = int(os.environ.get("KC1W", "3"))
KPHASE2 = int(os.environ.get("KPHASE2", "3"))
KL2DBG = int(os.environ.get("KL2DBG", "0"))
KQD = int(os.environ.get("KQD", "0"))
KBA = int(os.environ.get("KBA", "1"))
KBC = int(os.environ.get("KBC", "1"))
KBD = int(os.environ.get("KBD", "1"))
KBE = int(os.environ.get("KBE", "1"))


class S:
    def __init__(self, h, name):
        self.h, self.name, self.n = h, name, 0

    def inc(self, k=1):
        self.n += k
        return self.n


class Prog:

    def __init__(self):
        self.q = {e: [] for e in ("pe", "act", "dve", "pool", "sp")}
        self.junk = None

    def fence(self, eng, sem):
        self.settle(eng)
        meth = "engine_nop" if eng in ("dve", "pool") else "nop"
        self.op(eng, meth, (sem, sem.inc(1)))

    def settle(self, eng):
        self.q[eng].append(self.junk[eng])

    def emit(self, eng, fn):
        self.q[eng].append(fn)

    def wait(self, eng, sem, val):
        assert val <= sem.n, (
            f"deadlock: {eng} waits {sem.name}>={val}, emitted {sem.n}")
        if val > 0:
            self.q[eng].append(lambda E, s=sem, v=val: E.wait_ge(s.h, v))

    def op(self, eng, meth, inc=None, **kw):
        def f(E, meth=meth, kw=kw, inc=inc):
            ins = getattr(E, meth)(**kw)
            if inc is not None:
                ins.then_inc(inc[0].h, 1)
        self.emit(eng, f)

    def pwait(self, eng, sem, val):
        self.wait(eng, sem, val)
        self.settle(eng)

    def mm(self, out, lhsT, rhs, start, stop, inc=None):
        self.op("pe", "matmul", inc, out=out, lhsT=lhsT, rhs=rhs,
                start=start, stop=stop)

    def dma(self, queue, out, in_, sem, accum=False):
        v = sem.inc(16)
        def f(E, out=out, in_=in_, sem=sem, accum=accum):
            kw = dict(accum_op=ALU.add) if accum else {}
            E.dma_start(out=out, in_=in_, **kw).then_inc(sem.h, 16)
        self.emit(queue, f)
        return v

    def barrier(self, eng_sems):
        marks = {}
        for e, sem in eng_sems.items():
            meth = "engine_nop" if e in ("dve", "pool") else "nop"
            self.op(e, meth, (sem, sem.inc(1)))
            marks[e] = sem.n
        for e in eng_sems:
            for e2 in eng_sems:
                if e2 != e:
                    self.wait(e, eng_sems[e2], marks[e2])

    def run(self, nc):
        with nc.Block() as block:
            @block.tensor
            def _(E):
                for f in self.q["pe"]:
                    f(E)

            @block.scalar
            def _(E):
                for f in self.q["act"]:
                    f(E)

            @block.vector
            def _(E):
                for f in self.q["dve"]:
                    f(E)

            @block.gpsimd
            def _(E):
                for f in self.q["pool"]:
                    f(E)

            @block.sync
            def _(E):
                for f in self.q["sp"]:
                    f(E)


class DStream:
    _uid = 0

    def __init__(self, prog, nc, ctx, nslots=2):
        self.prog = prog
        self.s = []
        for i in range(nslots):
            DStream._uid += 1
            nm = f"ds{DStream._uid}"
            self.s.append(S(ctx.enter_context(nc.semaphore(nm)), nm))

    def load(self, queue, slot, pairs, gate=None, accum=False):
        if gate is not None:
            self.prog.wait(queue, gate[0], gate[1])
        v = None
        for (out, in_) in pairs:
            v = self.prog.dma(queue, out, in_, self.s[slot], accum=accum)
        return (slot, v)

    def wait(self, eng, tok):
        self.prog.wait(eng, self.s[tok[0]], tok[1])


def sbuf(ctx, nc, name, p, f, dt):
    return ctx.enter_context(nc.sbuf_tensor(name, [p, f], dt))


def psum(ctx, nc, name, p, f, dt=F32):
    return ctx.enter_context(nc.psum_tensor(name, [p, f], dt))


def tiled(ap_):
    return ap_.rearrange("(n p) t -> p n t", p=128)


def wtiled(ap_):
    return ap_.rearrange("(n p) c -> p n c", p=128)


def emit_ln(prog, G, x_f, lnx_flat, t0, tstride, n_d=NT, tcols=TC, eps=1e-5):
    pe, act, dve = G["pe"], G["act"], G["dve"]
    xe, x2, rows, rows_bf = G["xe"], G["x2"], G["rows"], G["rows_bf"]
    mb, rb, tmp = G["mb"], G["rb"], G["tmp"]
    ps_rows, ps_bc = G["ps_rows"], G["ps_bc"]
    pe_tile = []
    for n in range(n_d):
        sl = slice(n * tcols, (n + 1) * tcols)
        rot = slice((n % 2) * tcols, (n % 2) * tcols + tcols)
        if n >= 2:
            prog.wait("dve", pe, pe_tile[n - 2])
            prog.wait("act", pe, pe_tile[n - 2])
        prog.op("dve", "tensor_copy", (dve, dve.inc(1)),
                out=xe[:, rot], in_=x_f[:, sl])
        prog.op("act", "activation", (act, act.inc(1)),
                out=x2[:, rot], in_=x_f[:, sl], func=AF.Square)
        prog.wait("pe", dve, dve.n)
        prog.wait("pe", act, act.n)
        prog.mm(ps_rows[0:1, 0:tcols], G["ones_col"][:], xe[:, rot],
                n == 0, n == n_d - 1)
        prog.mm(ps_rows[0:1, tcols:2 * tcols], G["ones_col"][:], x2[:, rot],
                n == 0, n == n_d - 1, inc=(pe, pe.inc(1)))
        pe_tile.append(pe.n)
    r_mean = rows[0:1, 0:tcols]
    r_ex2 = rows[0:1, tcols:2 * tcols]
    r_var = rows[0:1, 2 * tcols:3 * tcols]
    r_sd = rows[0:1, 3 * tcols:4 * tcols]
    prog.wait("dve", pe, pe.n)
    prog.settle("dve")
    prog.op("dve", "tensor_scalar_mul", None, out=r_mean,
            in0=ps_rows[0:1, 0:tcols], scalar1=1.0 / (128 * n_d))
    prog.op("dve", "tensor_scalar_mul", None, out=r_ex2,
            in0=ps_rows[0:1, tcols:2 * tcols], scalar1=1.0 / (128 * n_d))
    prog.op("dve", "tensor_mul", None, out=r_var, in0=r_mean, in1=r_mean)
    prog.op("dve", "tensor_sub", None, out=r_var, in0=r_ex2, in1=r_var)
    prog.op("dve", "tensor_scalar_add", (dve, dve.inc(1)),
            out=r_var, in0=r_var, scalar1=eps)
    prog.wait("act", dve, dve.n)
    prog.op("act", "activation", (act, act.inc(1)),
            out=r_sd, in_=r_var, func=AF.Sqrt)
    prog.wait("dve", act, act.n)
    prog.op("dve", "reciprocal", None, out=r_var, in_=r_sd)
    prog.op("dve", "tensor_copy", None, out=rows_bf[0:1, 0:tcols], in_=r_mean)
    prog.op("dve", "tensor_copy", (dve, dve.inc(1)),
            out=rows_bf[0:1, tcols:2 * tcols], in_=r_var)
    prog.wait("pe", dve, dve.n)
    prog.mm(ps_bc[:, 0:tcols], G["ones_row"][:], rows_bf[0:1, 0:tcols],
            True, True)
    prog.mm(ps_bc[:, tcols:2 * tcols], G["ones_row"][:],
            rows_bf[0:1, tcols:2 * tcols], True, True, inc=(pe, pe.inc(1)))
    prog.wait("dve", pe, pe.n)
    prog.settle("dve")
    prog.op("dve", "tensor_copy", None, out=mb[:, 0:tcols],
            in_=ps_bc[:, 0:tcols])
    prog.op("dve", "tensor_copy", None, out=rb[:, 0:tcols],
            in_=ps_bc[:, tcols:2 * tcols])
    for n in range(n_d):
        sl = slice(n * tcols, (n + 1) * tcols)
        dsl = slice(n * tstride + t0, n * tstride + t0 + tcols)
        prog.op("dve", "tensor_sub", None, out=tmp[:, 0:tcols],
                in0=x_f[:, sl], in1=mb[:, 0:tcols])
        prog.op("dve", "tensor_mul",
                (dve, dve.inc(1)) if n == n_d - 1 else None,
                out=lnx_flat[:, dsl], in0=tmp[:, 0:tcols], in1=rb[:, 0:tcols])
    return dve.n


def dve_noop(prog, G):
    dve = G["dve"]
    prog.op("dve", "engine_nop", (dve, dve.inc(1)))
    return dve.n


def ln_scratch(P, nc, pfx, pe, act, dve, ones_col, ones_row):
    G = dict(pe=pe, act=act, dve=dve,
             xe=sbuf(P, nc, pfx + "xe", 128, 2 * TC, BF16),
             x2=sbuf(P, nc, pfx + "x2", 128, 2 * TC, BF16),
             rows=sbuf(P, nc, pfx + "rows", 1, 4 * TC, F32),
             rows_bf=sbuf(P, nc, pfx + "rows_bf", 1, 2 * TC, BF16),
             mb=sbuf(P, nc, pfx + "mb", 128, TC, F32),
             rb=sbuf(P, nc, pfx + "rb", 128, TC, F32),
             tmp=sbuf(P, nc, pfx + "tmp", 128, TC, F32),
             ps_bc=psum(P, nc, pfx + "psb", 128, 2 * TC),
             ones_col=ones_col, ones_row=ones_row)
    G["ps_rows"] = G["ps_bc"]
    return G


def emit_rmsnorm(prog, H, pslot, outb, w_col, extra_scale=None):
    pe, act, dve = H["pe"], H["act"], H["dve"]
    prog.wait("act", pe, pe.n)
    prog.settle("act")
    prog.op("act", "activation", (act, act.inc(1)),
            out=H["sq"][:], in_=pslot, func=AF.Square)
    prog.wait("pe", act, act.n)
    prog.wait("pe", dve, dve.n)
    prog.mm(H["ps_rs"][0:1, :], H["ones_col"][:], H["sq"][:], True, True,
            inc=(pe, pe.inc(1)))
    prog.wait("dve", pe, pe.n)
    prog.settle("dve")
    r_ms, r_sd = H["r_ms"], H["r_sd"]
    prog.op("dve", "tensor_scalar", (dve, dve.inc(1)), out=r_ms,
            in0=H["ps_rs"][0:1, :], scalar1=1.0 / HD, scalar2=1e-6,
            op0=ALU.mult, op1=ALU.add)
    prog.wait("act", dve, dve.n)
    prog.op("act", "activation", (act, act.inc(1)), out=r_sd, in_=r_ms,
            func=AF.Sqrt)
    prog.wait("dve", act, act.n)
    prog.op("dve", "reciprocal", None, out=r_ms, in_=r_sd)
    if extra_scale is not None:
        prog.op("dve", "tensor_scalar_mul", None, out=r_ms, in0=r_ms,
                scalar1=extra_scale)
    prog.op("dve", "tensor_copy", (dve, dve.inc(1)),
            out=H["row_bf"][0:1, :], in_=r_ms)
    prog.wait("pe", dve, dve.n)
    prog.mm(H["ps_rb"][:, :], H["ones_row"][:], H["row_bf"][0:1, :],
            True, True, inc=(pe, pe.inc(1)))
    prog.wait("dve", pe, pe.n)
    prog.settle("dve")
    prog.op("dve", "tensor_copy", None, out=H["rstd"][:], in_=H["ps_rb"][:, :])
    prog.op("dve", "scalar_tensor_tensor", None, out=outb[:],
            in0=H["rstd"][:], scalar=w_col, in1=pslot,
            op0=ALU.mult, op1=ALU.mult)


def emit_rope(prog, R, kn, cod, sid, out_t, out_cols):
    pe, dve = R["pe"], R["dve"]
    knb, ps_sw, ra = R["knb"], R["ps_sw"], R["ra"]
    prog.op("dve", "tensor_copy", (dve, dve.inc(1)), out=knb[:], in_=kn[:])
    prog.wait("pe", dve, dve.n)
    prog.mm(ps_sw[:, :], R["perm"][:], knb[:], True, True,
            inc=(pe, pe.inc(1)))
    prog.wait("dve", pe, pe.n)
    a1 = ra[:, 0:TC]
    a2 = ra[:, TC:2 * TC]
    prog.op("dve", "tensor_mul", None, out=a1[0:64, :], in0=kn[0:64, :],
            in1=cod[0:64, :])
    prog.op("dve", "tensor_mul", None, out=a2[0:64, :],
            in0=ps_sw[0:64, :], in1=sid[0:64, :])
    prog.op("dve", "tensor_sub", None, out=out_t[0:64, out_cols],
            in0=a1[0:64, :], in1=a2[0:64, :])
    prog.op("dve", "tensor_mul", None, out=a1[64:128, :],
            in0=ps_sw[64:128, :], in1=sid[64:128, :])
    prog.op("dve", "tensor_mul", None, out=a2[64:128, :],
            in0=kn[64:128, :], in1=cod[64:128, :])
    prog.op("dve", "tensor_add", (dve, dve.inc(1)),
            out=out_t[64:128, out_cols], in0=a1[64:128, :],
            in1=a2[64:128, :])


def build_launch1():
    nc = bass.Bass("TRN2", target_bir_lowering=False,
                   detect_race_conditions=bool(int(
                       os.environ.get("KRACE", "0"))))

    def din(name, shape, dt=F32):
        return nc.dram_tensor(name, list(shape), dt, kind="ExternalInput").ap()

    def dout(name, shape, dt=F32):
        return nc.dram_tensor(name, list(shape), dt, kind="ExternalOutput").ap()

    def dint(name, shape, dt=F32):
        kind = "ExternalOutput" if name in DBG else "Internal"
        return nc.dram_tensor(name, list(shape), dt, kind=kind).ap()

    xT = din("xT", (D, TEXT))
    wqT = din("wqT", (D, D))
    wkT = din("wkT", (D, NKV * HD))
    wvT = din("wvT", (D, NKV * HD))
    woT = din("woT", (D, D))
    wgT = din("wgT", (D, FFH))
    wuT = din("wuT", (D, FFH))
    wdT = din("wdT", (FFH, D))
    l1kT = din("l1kT", (D, D))
    l1vT = din("l1vT", (D, D))
    l1rT = din("l1rT", (D, D))
    cosT = din("cosT", (128, TEXT))
    sinT = din("sinT", (128, TEXT))
    qnw = din("qnw", (128, 1))
    knw = din("knw", (128, 1))
    masksIn = din("masksIn", (16, 128, KSPAN), BF16)
    tdT = din("tdT", (128, NT))
    tfT = din("tfT", (128, NT))
    isf = din("isf", (128, 2))
    identIn = din("identIn", (128, 128), BF16)
    permIn = din("permIn", (128, 128), BF16)
    onesColIn = din("onesColIn", (128, 1), BF16)
    onesRowIn = din("onesRowIn", (1, 128), BF16)

    resid2 = dout("resid2", (D, TLOC))
    l1k = dout("l1k", (D, TLOC))
    l1v = dout("l1v", (D, TLOC))
    rsig = dout("rsig", (D, TLOC), BF16)
    sfin = dout("sfin", (128, 2 * NT))
    resid1 = dint("resid1", (D, TLOC))
    lnx_sp = dint("lnx_sp", (D, TEXT), BF16)
    qrot_sp = dint("qrot_sp", (NH * HD, TLOC), BF16)

    xT_t, lnx_t = tiled(xT), tiled(lnx_sp)
    r1_t, r2_t = tiled(resid1), tiled(resid2)
    l1k_t, l1v_t, rsig_t = tiled(l1k), tiled(l1v), tiled(rsig)
    qrot_t = tiled(qrot_sp)

    prog = Prog()
    with ExitStack() as ctx:
        sems = {}
        for nm in ("pe", "act", "dve", "pool", "spq"):
            sems[nm] = S(ctx.enter_context(nc.semaphore(nm)), nm)
        pe, act, dve = sems["pe"], sems["act"], sems["dve"]
        eng = {"pe": pe, "act": act, "dve": dve, "pool": sems["pool"],
               "sp": sems["spq"]}

        jkd = sbuf(ctx, nc, "jkd", 128, 1, F32)
        jkdo = sbuf(ctx, nc, "jkdo", 128, 1, F32)
        jka = sbuf(ctx, nc, "jka", 128, 1, F32)
        jkao = sbuf(ctx, nc, "jkao", 128, 512, F32)
        prog.junk = {
            "dve": lambda E: E.reduce_sum(
                out=jkdo[:], in_=jkd[:, 0:1].broadcast_to([128, 512]),
                axis=AX.X),
            "act": lambda E: E.activation(
                out=jkao[:], in_=jka[:, 0:1].broadcast_to([128, 512]),
                func=AF.Copy),
        }
        prog.emit("dve", lambda E: E.memzero(jkd[:]))
        prog.emit("act", lambda E: E.memzero(jka[:]))
        globals()["_JKA"] = jka
        jmv = sbuf(ctx, nc, "jmv", 128, 8, BF16)
        prog.emit("dve", lambda E: E.memzero(jmv[:]))

        def _flush(junk):
            prog.mm(junk, ones_col[:], jmv[:], True, True,
                    inc=(pe, pe.inc(1)))
        prog.flush = _flush
        ident = sbuf(ctx, nc, "ident", 128, 128, BF16)
        perm = sbuf(ctx, nc, "perm", 128, 128, BF16)
        ones_col = sbuf(ctx, nc, "ones_col", 128, 1, BF16)
        ones_row = sbuf(ctx, nc, "ones_row", 1, 128, BF16)
        qnw_sb = sbuf(ctx, nc, "qnw_sb", 128, 1, F32)
        knw_sb = sbuf(ctx, nc, "knw_sb", 128, 1, F32)
        isf_sb = sbuf(ctx, nc, "isf_sb", 128, 2, F32)
        tdT_sb = sbuf(ctx, nc, "tdT_sb", 128, NT, F32)
        tfT_sb = sbuf(ctx, nc, "tfT_sb", 128, NT, F32)
        dsC = DStream(prog, nc, ctx, 1)
        tok = dsC.load("sp", 0, [
            (ident[:], identIn[:]), (perm[:], permIn[:]),
            (ones_col[:], onesColIn[:]),
            (ones_row[:], onesRowIn[:]), (qnw_sb[:], qnw[:]),
            (knw_sb[:], knw[:]), (isf_sb[:], isf[:]),
            (tdT_sb[:], tdT[:]), (tfT_sb[:], tfT[:])])
        for e in ("pe", "act", "dve", "pool"):
            dsC.wait(e, tok)

        actx = ExitStack()
        krot = sbuf(actx, nc, "krot", 128, NKV * TEXT, BF16)
        vtok = sbuf(actx, nc, "vtok", 128, (TEXT // 128) * 512, BF16)
        with ExitStack() as P:
            xbuf = sbuf(P, nc, "xbuf", 128, NT * TC, F32)
            lnx = sbuf(P, nc, "lnx", 128, NT * TC, BF16)
            wk_sb = sbuf(P, nc, "wk_sb", 128, NT * 512, BF16)
            wv_sb = sbuf(P, nc, "wv_sb", 128, NT * 512, BF16)
            cs_sb = sbuf(P, nc, "cs_sb", 128, 2 * TC, F32)
            kn = sbuf(P, nc, "kn", 128, TC, F32)
            knb = sbuf(P, nc, "knb", 128, TC, BF16)
            ra = sbuf(P, nc, "ra", 128, 2 * TC, F32)
            G = ln_scratch(P, nc, "a", pe, act, dve, ones_col, ones_row)
            H = dict(pe=pe, act=act, dve=dve, ones_col=ones_col,
                     ones_row=ones_row,
                     sq=sbuf(P, nc, "asq", 128, TC, BF16),
                     rstd=sbuf(P, nc, "arstd", 128, TC, F32),
                     row_bf=sbuf(P, nc, "arow_bf", 1, TC, BF16),
                     r_ms=sbuf(P, nc, "ams", 1, TC, F32)[0:1, :],
                     r_sd=sbuf(P, nc, "asd", 1, TC, F32)[0:1, :],
                     ps_rb=psum(P, nc, "aps_rb", 128, TC))
            H["ps_rs"] = H["ps_rb"]
            ps_pj_t = [psum(P, nc, "aps_pj0", 128, TC),
                       psum(P, nc, "aps_pj1", 128, TC)]
            ps_sw = psum(P, nc, "aps_sw", 128, TC)
            ps_jk = psum(P, nc, "aps_jk", 1, 8)

            dsW = DStream(prog, nc, ctx, 1)
            tokW = dsW.load("pool", 0, [
                (wk_sb[:].rearrange("p (n c) -> p n c", n=NT),
                 wtiled(wkT)[:, :, :]),
                (wv_sb[:].rearrange("p (n c) -> p n c", n=NT),
                 wtiled(wvT)[:, :, :])])
            dsW.wait("pe", tokW)
            if int(os.environ.get("KDUMPW", "0")):
                dsW.wait("sp", tokW)
                _dst = rsig.rearrange("(a p) t -> p a t", p=128)
                prog.dma("sp", _dst[:, 0:4, :],
                         wk_sb[:].rearrange("p (a t) -> p a t", a=4),
                         dsW.s[0])
                prog.dma("sp", _dst[:, 4:8, :],
                         wv_sb[:].rearrange("p (a t) -> p a t", a=4),
                         dsW.s[0])

            dsX = DStream(prog, nc, ctx, 1)
            dsS = DStream(prog, nc, ctx, 1)
            spill_tok = None
            for ci in range(min(NCH, NCH_LIM)):
                t0 = ci * TC
                tokX = dsX.load("sp", 0, [
                    (xbuf[:].rearrange("p (n t) -> p n t", n=NT),
                     xT_t[:, :, t0:t0 + TC]),
                    (cs_sb[:, 0:TC], cosT[:, t0:t0 + TC]),
                    (cs_sb[:, TC:2 * TC], sinT[:, t0:t0 + TC])],
                    gate=(dve, dve.n) if ci > 0 else None)
                if ci > 0:
                    prog.wait("sp", act, act.n)
                    prog.wait("sp", pe, pe.n)
                dsX.wait("dve", tokX)
                dsX.wait("act", tokX)
                dsX.wait("pe", tokX)
                if ci > 0:
                    prog.wait("dve", pe, pe.n)
                    prog.wait("act", pe, pe.n)
                    dsS.wait("dve", spill_tok)
                if SUB_A1A < 2:
                    continue
                emit_ln(prog, G, xbuf, lnx, 0, TC)
                lnx_ready = dve.n
                if SUB_A1A < 3:
                    continue
                prog.wait("sp", dve, lnx_ready)
                spill_tok = dsS.load("sp", 0, [
                    (lnx_t[:, :, t0:t0 + TC],
                     lnx[:].rearrange("p (n t) -> p n t", n=NT))])
                prog.wait("pe", dve, lnx_ready)
                if SUB_A1A < 4:
                    continue
                for h in range(NKV):
                    slot = h % 2
                    pslot = ps_pj_t[slot][:, :]
                    prog.wait("pe", dve, dve.n)
                    prog.wait("pe", act, act.n)
                    for n in range(NT):
                        prog.mm(pslot,
                                wk_sb[:, n * 512 + h * 128:n * 512 + h * 128 + 128],
                                lnx[:, n * TC:(n + 1) * TC], n == 0, n == NT - 1,
                                inc=(pe, pe.inc(1)) if n == NT - 1 else None)
                    emit_rmsnorm(prog, H, pslot, kn, knw_sb[:, 0:1])
                    if SUB_A1A < 5:
                        continue
                    dsX.wait("dve", tokX)
                    emit_rope(prog, dict(pe=pe, dve=dve, knb=knb,
                                         ps_sw=ps_sw, ra=ra, perm=perm),
                              kn, cs_sb[:, 0:TC], cs_sb[:, TC:2 * TC], krot,
                              slice(h * TEXT + t0, h * TEXT + t0 + TC))
                for ts in range(TC // 128):
                    slot = ts % 2
                    pslot = ps_pj_t[slot][:, :]
                    prog.wait("pe", act, act.n)
                    prog.wait("pe", dve, dve.n)
                    for n in range(NT):
                        prog.mm(pslot,
                                lnx[:, n * TC + ts * 128:
                                    n * TC + ts * 128 + 128],
                                wv_sb[:, n * 512:(n + 1) * 512],
                                n == 0, n == NT - 1,
                                inc=(pe, pe.inc(1)) if n == NT - 1 else None)
                    tt = (t0 + ts * 128) // 128
                    prog.flush(ps_jk[0:1, :])
                    prog.wait("dve", pe, pe.n)
                    prog.settle("dve")
                    prog.op("dve", "tensor_copy", None,
                            out=H["sq"][:], in_=pslot)
                    prog.op("dve", "tensor_copy", (dve, dve.inc(1)),
                            out=vtok[:, tt * 512:(tt + 1) * 512],
                            in_=H["sq"][:])
            dsS.wait("sp", spill_tok)
        prog.barrier(eng)

        if PHASE_LIMIT >= 2:
          with ExitStack() as P:
            lnxq = sbuf(P, nc, "lnxq", 128, NT * TC, BF16)
            wq_sb = sbuf(P, nc, "wq_sb", 128, 2 * NT * 128, BF16)
            qro = sbuf(P, nc, "qro", 128, 2 * TC, BF16)
            qn = sbuf(P, nc, "qn", 128, TC, F32)
            qnb = sbuf(P, nc, "qnb", 128, TC, BF16)
            raq = sbuf(P, nc, "raq", 128, 2 * TC, F32)
            csq = sbuf(P, nc, "csq", 128, 2 * TC, F32)
            H = dict(pe=pe, act=act, dve=dve, ones_col=ones_col,
                     ones_row=ones_row,
                     sq=sbuf(P, nc, "bsq", 128, TC, BF16),
                     rstd=sbuf(P, nc, "brstd", 128, TC, F32),
                     row_bf=sbuf(P, nc, "brow_bf", 1, TC, BF16),
                     r_ms=sbuf(P, nc, "brows", 1, 2 * TC, F32)[0:1, 0:TC],
                     r_sd=None,
                     ps_rb=psum(P, nc, "bps_rb", 128, TC))
            H["ps_rs"] = H["ps_rb"]
            H["r_sd"] = sbuf(P, nc, "brows2", 1, TC, F32)[0:1, 0:TC]
            ps_q = psum(P, nc, "bps_q", 128, 2 * TC)
            ps_swq = psum(P, nc, "bps_sw", 128, TC)
            dsL = DStream(prog, nc, ctx, 1)
            dsW = DStream(prog, nc, ctx, 2)
            dsS = DStream(prog, nc, ctx, 2)
            wq_t = wtiled(wqT)
            st_tok = {}
            qpd = {}
            for qc in range(NCO):
                t0 = HALO + qc * TC
                tokL = dsL.load("sp", 0, [
                    (lnxq[:].rearrange("p (n t) -> p n t", n=NT),
                     lnx_t[:, :, t0:t0 + TC]),
                    (csq[:, 0:TC], cosT[:, t0:t0 + TC]),
                    (csq[:, TC:2 * TC], sinT[:, t0:t0 + TC])],
                    gate=(dve, dve.n) if qc > 0 else None)
                dsL.wait("pe", tokL)
                dsL.wait("dve", tokL)
                for h in range(NH):
                    it = qc * NH + h
                    ws = it % 2
                    wsl = slice(ws * NT * 128, (ws + 1) * NT * 128)
                    tokW = dsW.load("pool", ws, [
                        (wq_sb[:, wsl].rearrange("p (n c) -> p n c", n=NT),
                         wq_t[:, :, h * 128:(h + 1) * 128])],
                        gate=(pe, qpd[it - 2]) if it >= 2 else None)
                    dsW.wait("pe", tokW)
                    pslot = ps_q[:, (it % 2) * TC:(it % 2) * TC + TC]
                    prog.wait("pe", dve, dve.n)
                    for n in range(NT):
                        prog.mm(pslot,
                                wq_sb[:, ws * NT * 128 + n * 128:
                                      ws * NT * 128 + n * 128 + 128],
                                lnxq[:, n * TC:(n + 1) * TC], n == 0, n == NT - 1,
                                inc=(pe, pe.inc(1)) if n == NT - 1 else None)
                    qpd[it] = pe.n
                    emit_rmsnorm(prog, H, pslot, qn, qnw_sb[:, 0:1],
                                 extra_scale=1.0 / SQH)
                    if it >= 2:
                        dsS.wait("dve", st_tok[it - 2])
                    emit_rope(prog, dict(pe=pe, dve=dve, knb=qnb,
                                         ps_sw=ps_swq, ra=raq, perm=perm),
                              qn, csq[:, 0:TC], csq[:, TC:2 * TC], qro,
                              slice((it % 2) * TC, (it % 2) * TC + TC))
                    prog.wait("sp", dve, dve.n)
                    st_tok[it] = dsS.load("sp", it % 2, [
                        (qrot_t[:, h, qc * TC:(qc + 1) * TC],
                         qro[:, (it % 2) * TC:(it % 2) * TC + TC])])
            for itx in (NCO * NH - 2, NCO * NH - 1):
                dsS.wait("sp", st_tok[itx])
          prog.barrier(eng)

        if PHASE_LIMIT >= 3:
          with ExitStack() as P:
            qrot = sbuf(P, nc, "qrot", 128, NH * TC, BF16)
            masks = sbuf(P, nc, "masks", 128, 4 * KSPAN, BF16)
            pbuf = sbuf(P, nc, "pbuf", 128, 2 * KSPAN, BF16)
            pmk = sbuf(P, nc, "pmk", 128, 2 * KSPAN, BF16)
            psc = sbuf(P, nc, "psc", 128, 2 * KSPAN, BF16)
            ptr = sbuf(P, nc, "ptr", 128, 2 * KSPAN, BF16)
            nmax = sbuf(P, nc, "nmax", 128, 4, F32)
            dsum = sbuf(P, nc, "dsum", 128, 4, F32)
            rden = sbuf(P, nc, "rden", 128, 4, F32)
            av_sb = sbuf(P, nc, "av_sb", 128, NH * TC, BF16)
            avstage = sbuf(P, nc, "avstage", 128, TC, F32)
            wo_sb = sbuf(P, nc, "wo_sb", 128, 2 * NT * 128, BF16)
            rbuf = sbuf(P, nc, "rbuf", 128, 2 * TC, F32)
            oev = sbuf(P, nc, "oev", 128, 2 * TC, F32)
            ps_sc = psum(P, nc, "ps_sc", 128, KSPAN + 8)
            ps_tr = psum(P, nc, "ps_tr", 128, KSPAN, BF16)
            ps_av = psum(P, nc, "ps_av", 128, TC)
            ps_o = psum(P, nc, "ps_o", 128, 2 * 512)
            masks_r = masksIn.rearrange("q p k -> p q k")
            wo_t = wtiled(woT)
            dsQ = DStream(prog, nc, ctx, 1)
            dsM = DStream(prog, nc, ctx, 1)
            dsW = DStream(prog, nc, ctx, 2)
            dsR = DStream(prog, nc, ctx, 2)
            dsS = DStream(prog, nc, ctx, 2)
            opd, oed, std = {}, {}, {}
            for qc in range(NCO):
                tokQ = dsQ.load("sp", 0, [
                    (qrot[:].rearrange("p (n t) -> p n t", n=NH),
                     qrot_t[:, :, qc * TC:(qc + 1) * TC])],
                    gate=(pe, pe.n) if qc > 0 else None)
                dsQ.wait("pe", tokQ)
                tokM = dsM.load("sp", 0,
                                [(masks[:], masks_r[:, 4 * qc:4 * qc + 4, :])],
                                gate=(dve, dve.n) if qc > 0 else None)
                dsM.wait("dve", tokM)
                NIT = min(NH * 4, KNIT)
                exp_d, stt_d, scl_d, tr_d, ev_d, av_d = {}, {}, {}, {}, {}, {}
                avx_d, avv_d = {}, {}
                for it in range(NIT + 1):
                    if it < NIT:
                        h, qb = it // 4, it % 4
                        par = it % 2
                        gq = qc * 4 + qb
                        kv = h // 4
                        kc0 = kv * TEXT + gq * 128
                        qsl = slice(h * TC + qb * 128,
                                    h * TC + qb * 128 + 128)
                        psl = slice(par * KSPAN, par * KSPAN + KSPAN)
                        if it >= 1:
                            prog.wait("pe", act, exp_d[it - 1])
                            if KBA and KA2S2 and it >= 2 and (it - 2) % 4 == 3:
                                prog.wait("pe", dve, avv_d[(it - 2) // 4])
                        elif qc > 0:
                            prog.wait("pe", act, act.n)
                            prog.wait("pe", dve, dve.n)
                        for j, (o0, w) in enumerate(((0, 512), (512, 512),
                                                     (1024, 128))):
                            prog.mm(ps_sc[:, o0:o0 + w], qrot[:, qsl],
                                    krot[:, kc0 + o0:kc0 + o0 + w], True, True,
                                    inc=(pe, pe.inc(1)) if j == 2 else None)
                        prog.flush(ps_sc[0:1, KSPAN:KSPAN + 8])
                        sc_d = pe.n
                        prog.wait("dve", pe, sc_d)
                        prog.settle("dve")
                        prog.op("dve", "reduce_max", (dve, dve.inc(1)),
                                out=nmax[:, par:par + 1],
                                in_=ps_sc[:, 0:KSPAN],
                                axis=AX.X, negate=True)
                        nm_d = dve.n
                        prog.wait("act", dve, nm_d)
                        if it >= 2:
                            prog.wait("act", dve, stt_d[it - 2])
                        prog.op("act", "activation", (act, act.inc(1)),
                                out=pbuf[:, psl], in_=ps_sc[:, 0:KSPAN],
                                func=AF.Exp,
                                bias=nmax[:, par:par + 1])
                        exp_d[it] = act.n
                        prog.wait("dve", act, exp_d[it])
                        if it >= 2:
                            prog.wait("dve", act, scl_d[it - 2])
                        prog.op("dve", "scalar_tensor_tensor", None,
                                out=pmk[:, psl], in0=pbuf[:, psl], scalar=1.0,
                                in1=masks[:, qb * KSPAN:(qb + 1) * KSPAN],
                                op0=ALU.mult, op1=ALU.mult,
                                accum_out=dsum[:, par:par + 1])
                        prog.settle("dve")
                        prog.op("dve", "reciprocal", (dve, dve.inc(1)),
                                out=rden[:, par:par + 1],
                                in_=dsum[:, par:par + 1])
                        stt_d[it] = dve.n
                        prog.wait("act", dve, stt_d[it])
                        if it >= 2 and KA2S2:
                            prog.wait("act", pe, tr_d[it - 2])
                        prog.op("act", "activation", (act, act.inc(1)),
                                out=psc[:, psl], in_=pmk[:, psl], func=AF.Copy,
                                scale=rden[:, par:par + 1])
                        scl_d[it] = act.n
                    if it >= 1 and KA2S2:
                        jt = it - 1
                        h, qb = jt // 4, jt % 4
                        par = jt % 2
                        gq = qc * 4 + qb
                        kv = h // 4
                        psl = slice(par * KSPAN, par * KSPAN + KSPAN)
                        prog.wait("pe", act, scl_d[jt])
                        if KB2 and it < NIT:
                            prog.wait("pe", act, exp_d[it])
                        if jt >= 1:
                            prog.wait("pe", dve, ev_d[jt - 1])
                        elif qc > 0:
                            prog.wait("pe", dve, dve.n)
                        for j in range(9):
                            prog.op("pe", "transpose",
                                    (pe, pe.inc(1)) if j == 8 else None,
                                    out=ps_tr[:, j * 128:(j + 1) * 128],
                                    in_=psc[:, par * KSPAN + j * 128:
                                            par * KSPAN + (j + 1) * 128],
                                    identity=ident[:])
                        prog.flush(ps_sc[0:1, KSPAN:KSPAN + 8])
                        tr_d[jt] = pe.n
                        prog.wait("dve", pe, tr_d[jt])
                        if jt >= 2:
                            prog.wait("dve", pe, av_d[jt - 2])
                        prog.settle("dve")
                        prog.op("dve", "tensor_copy", (dve, dve.inc(1)),
                                out=ptr[:, par * KSPAN:(par + 1) * KSPAN],
                                in_=ps_tr[:, :])
                        ev_d[jt] = dve.n
                        prog.wait("pe", dve, ev_d[jt])
                        if qb == 0 and h >= 1:
                            prog.wait("pe", dve, avv_d[h - 1])
                        elif jt == 0:
                            prog.wait("pe", act, act.n)
                        for j in range(9):
                            tglob = gq + j
                            prog.mm(ps_av[:, qb * 128:(qb + 1) * 128],
                                    vtok[:, tglob * 512 + kv * 128:
                                         tglob * 512 + kv * 128 + 128],
                                    ptr[:, par * KSPAN + j * 128:
                                        par * KSPAN + (j + 1) * 128],
                                    j == 0, j == 8,
                                    inc=(pe, pe.inc(1)) if j == 8 else None)
                        av_d[jt] = pe.n
                        if qb == 3:
                            prog.flush(ps_sc[0:1, KSPAN:KSPAN + 8])
                            prog.wait("dve", pe, pe.n)
                            prog.settle("dve")
                            prog.op("dve", "tensor_copy", (dve, dve.inc(1)),
                                    out=avstage[:], in_=ps_av[:, :])
                            prog.wait("act", dve, dve.n)
                            prog.op("act", "activation", (act, act.inc(1)),
                                    out=av_sb[:, h * TC:(h + 1) * TC],
                                    in_=avstage[:], func=AF.Copy)
                            avx_d[h] = act.n
                            avv_d[h] = dve.n
                av_all = act.n
                prog.wait("pe", act, av_all)
                prog.wait("pe", dve, dve.n)
                for ot in range(NT) if KA2S2 else []:
                    go = qc * NT + ot
                    osl = ot % 2
                    wsl = slice(osl * NT * 128, (osl + 1) * NT * 128)
                    tokW = dsW.load("pool", osl, [
                        (wo_sb[:, wsl].rearrange("p (n c) -> p n c", n=NT),
                         wo_t[:, :, ot * 128:(ot + 1) * 128])],
                        gate=(pe, opd[go - 2]) if go >= 2 else None)
                    tokR = dsR.load("sp", osl, [
                        (rbuf[:, osl * TC:(osl + 1) * TC],
                         xT_t[:, ot, HALO + qc * TC:HALO + (qc + 1) * TC])],
                        gate=(dve, oed[go - 2]) if go >= 2 else None)
                    dsW.wait("pe", tokW)
                    if go >= 1:
                        prog.wait("pe", dve, oed[go - 1])
                    else:
                        prog.wait("pe", dve, dve.n)
                    for n in range(NT):
                        prog.mm(ps_o[:, osl * 512:osl * 512 + TC],
                                wo_sb[:, osl * NT * 128 + n * 128:
                                      osl * NT * 128 + n * 128 + 128],
                                av_sb[:, n * TC:(n + 1) * TC], n == 0,
                                n == NT - 1,
                                inc=(pe, pe.inc(1)) if n == NT - 1 else None)
                    prog.flush(ps_sc[0:1, KSPAN:KSPAN + 8])
                    opd[go] = pe.n
                    prog.wait("dve", pe, opd[go])
                    dsR.wait("dve", tokR)
                    if go >= 2:
                        dsS.wait("dve", std[go - 2])
                    prog.settle("dve")
                    prog.op("dve", "tensor_add", (dve, dve.inc(1)),
                            out=oev[:, osl * TC:(osl + 1) * TC],
                            in0=ps_o[:, osl * 512:osl * 512 + TC],
                            in1=rbuf[:, osl * TC:(osl + 1) * TC])
                    oed[go] = dve.n
                    prog.wait("sp", dve, oed[go])
                    std[go] = dsS.load("sp", osl, [
                        ((r2_t if K1OUT else r1_t)[:, ot, qc * TC:(qc + 1) * TC],
                         oev[:, osl * TC:(osl + 1) * TC])])
                for otx in ((qc * NT + NT - 2, qc * NT + NT - 1)
                            if KA2S2 else ()):
                    dsS.wait("sp", std[otx])
                if KDUMPA and qc == 0:
                    scdump = sbuf(P, nc, "scdump", 128, KSPAN, F32)
                    prog.flush(ps_sc[0:1, KSPAN:KSPAN + 8])
                    prog.wait("dve", pe, pe.n)
                    prog.settle("dve")
                    prog.op("dve", "tensor_copy", (dve, dve.inc(1)),
                            out=scdump[:], in_=ps_sc[:, 0:KSPAN])
                    prog.wait("sp", dve, dve.n)
                    prog.wait("sp", act, act.n)
                    prog.wait("sp", pe, pe.n)
                    dsDbg = DStream(prog, nc, ctx, 1)
                    tokD = dsDbg.load("sp", 0, [
                        (l1k_t[:, 1, 0:KSPAN], scdump[:]),
                        (l1k_t[:, 0, 0:4], nmax[:, 0:4]),
                        (l1k_t[:, 0, 8:12], dsum[:, 0:4]),
                        (l1k_t[:, 0, 16:20], rden[:, 0:4]),
                        (rsig_t[:, 0:2, 0:KSPAN],
                         pbuf[:].rearrange("p (a k) -> p a k", a=2)),
                        (rsig_t[:, 2:4, 0:KSPAN],
                         pmk[:].rearrange("p (a k) -> p a k", a=2)),
                        (rsig_t[:, 4:6, 0:KSPAN],
                         psc[:].rearrange("p (a k) -> p a k", a=2)),
                        (rsig_t[:, 6:8, 0:KSPAN],
                         ptr[:].rearrange("p (a k) -> p a k", a=2)),
                        (rsig_t[:, 8:12, :],
                         av_sb[:].rearrange("p (a k) -> p a k", a=4)),
                        (rsig_t[:, 12:16, :],
                         qrot[:].rearrange("p (a k) -> p a k", a=4)),
                        ])
                    tokD2 = dsDbg.load("pool", 0, [
                        (l1v_t[:, 0:3, 0:1536],
                         masks[:].rearrange("p (a k) -> p a k", a=3)),
                        (l1v_t[:, 4:10, :],
                         krot[:].rearrange("p (a k) -> p a k", a=6)),
                        (l1v_t[:, 10:16, :],
                         vtok[:].rearrange("p (a k) -> p a k", a=6))])
                    dsDbg.wait("sp", tokD)
                    dsDbg.wait("sp", tokD2)
                    dsDbg.wait("pool", tokD)
          prog.barrier(eng)
        actx.close()

        if PHASE_LIMIT >= 4:
            emit_ffn(prog, nc, ctx, eng, src_t=r1_t, dst_t=r2_t,
                 wgT_t=wtiled(wgT), wuT_t=wtiled(wuT), wdT_t=wtiled(wdT),
                     scales=None, ones_col=ones_col, ones_row=ones_row,
                     pfx="f")
            prog.barrier(eng)

        if PHASE_LIMIT >= 5:
            emit_l1kvr(prog, nc, ctx, eng, r2_t=r2_t, l1kT=l1kT, l1vT=l1vT,
                       l1rT=l1rT, l1k_t=l1k_t, l1v_t=l1v_t, rsig_t=rsig_t,
                       ones_col=ones_col, ones_row=ones_row)
            prog.barrier(eng)

        if PHASE_LIMIT >= 6:
            emit_finals(prog, nc, ctx, eng, l1k_t=l1k_t, l1v_t=l1v_t,
                        tdT_sb=tdT_sb, tfT_sb=tfT_sb, isf_sb=isf_sb,
                        sfin=sfin)
            prog.barrier(eng)
        prog.run(nc)
    return nc


def emit_ffn(prog, nc, ctx, eng, src_t, dst_t, wgT_t, wuT_t, wdT_t, scales,
             ones_col, ones_row, pfx, resid_t=None, final_dst_t=None):
    pe, act, dve = eng["pe"], eng["act"], eng["dve"]
    int8 = scales is not None
    if int8:
        sg, su, sd = scales
    NQ = 8
    HQ = FFH // NQ
    NSUB = HQ // 128
    with ExitStack() as P:
        lnx2 = sbuf(P, nc, pfx + "ln", 128, NT * TLOC, BF16)
        with ExitStack() as PL:
            xbuf = sbuf(PL, nc, pfx + "xb", 128, NT * TC, F32)
            G = ln_scratch(PL, nc, pfx, pe, act, dve, ones_col, ones_row)
            dsX = DStream(prog, nc, ctx, 1)
            for cci in range(NCO):
                t0 = cci * TC
                tokX = dsX.load("sp", 0, [
                    (xbuf[:].rearrange("p (n t) -> p n t", n=NT),
                     src_t[:, :, t0:t0 + TC])],
                    gate=(dve, dve.n) if cci > 0 else None)
                if cci > 0:
                    prog.wait("sp", act, act.n)
                    prog.wait("sp", pe, pe.n)
                dsX.wait("dve", tokX)
                dsX.wait("act", tokX)
                if cci > 0:
                    prog.wait("dve", pe, pe.n)
                    prog.wait("act", pe, pe.n)
                emit_ln(prog, G, xbuf, lnx2, t0, TLOC)
            prog.wait("pe", dve, dve.n)
        if int8 and KPHASE2 == 5:
            prog.wait("sp", dve, dve.n)
            prog.wait("pool", dve, dve.n)
            dsDm3 = DStream(prog, nc, ctx, 1)
            tokD3 = dsDm3.load("pool", 0, [
                (final_dst_t[:, :, :],
                 lnx2[:].rearrange("p (n t) -> p n t", n=NT))])
            dsDm3.wait("sp", tokD3)
            dsDm3.wait("pool", tokD3)
        gated = sbuf(P, nc, pfx + "gt", 128, NSUB * TLOC, BF16)
        wg_sb = sbuf(P, nc, pfx + "wg", 128, 2 * NT * 256, BF16)
        wu_sb = sbuf(P, nc, pfx + "wu", 128, 2 * NT * 256, BF16)
        wd_sb = sbuf(P, nc, pfx + "wd", 128, 2 * NSUB * 128, BF16)
        sgl = sbuf(P, nc, pfx + "sgl", 128, TC, BF16)
        gstage = sbuf(P, nc, pfx + "gst", 128, TC, BF16)
        rbuf = sbuf(P, nc, pfx + "rbu", 128, 2 * TC, F32)
        ebuf = sbuf(P, nc, pfx + "ebu", 128, 2 * TC, F32)
        xbuf = sbuf(P, nc, pfx + "xr", 128, 2 * TC, F32)
        ps_g = psum(P, nc, pfx + "pg", 128, TC)
        ps_u = psum(P, nc, pfx + "pu", 128, TC)
        ps_d = psum(P, nc, pfx + "pd", 128, 2 * TC)
        ps_jk = psum(P, nc, pfx + "pjk", 1, 512)
        dsGU = DStream(prog, nc, ctx, 2)
        dsD = DStream(prog, nc, ctx, 2)
        dsR = DStream(prog, nc, ctx, 2)
        dsS = DStream(prog, nc, ctx, 2)

        for hq in range(NQ):
            gu_d = {}
            for hc in range(4):
                ws = hc % 2
                h0 = hq * HQ + hc * 256
                wsl = slice(ws * NT * 256, (ws + 1) * NT * 256)
                tokGU = dsGU.load("pool", ws, [
                    (wg_sb[:, wsl].rearrange("p (n c) -> p n c", n=NT),
                     wgT_t[:, :, h0:h0 + 256]),
                    (wu_sb[:, wsl].rearrange("p (n c) -> p n c", n=NT),
                     wuT_t[:, :, h0:h0 + 256])],
                    gate=(pe, gu_d[hc - 2]) if hc >= 2 else
                         ((pe, pe.n) if hq > 0 else (dve, dve.n)))
                dsGU.wait("pe", tokGU)
                for hs in range(2):
                    hh = (h0 + hs * 128) // 128
                    hloc = hh % NSUB
                    for tcI in range(NCO):
                        prog.wait("pe", dve, dve.n)
                        prog.wait("pe", act, act.n)
                        for n in range(NT):
                            prog.mm(ps_g[:, :],
                                    wg_sb[:, ws * NT * 256 + n * 256 +
                                          hs * 128:ws * NT * 256 + n * 256 +
                                          hs * 128 + 128],
                                    lnx2[:, n * TLOC + tcI * TC:
                                         n * TLOC + (tcI + 1) * TC],
                                    n == 0, n == NT - 1)
                        for n in range(NT):
                            prog.mm(ps_u[:, :],
                                    wu_sb[:, ws * NT * 256 + n * 256 +
                                          hs * 128:ws * NT * 256 + n * 256 +
                                          hs * 128 + 128],
                                    lnx2[:, n * TLOC + tcI * TC:
                                         n * TLOC + (tcI + 1) * TC],
                                    n == 0, n == NT - 1,
                                    inc=(pe, pe.inc(1)) if n == NT - 1 else None)
                        prog.flush(ps_jk[0:1, 0:8])
                        prog.wait("act", pe, pe.n)
                        prog.settle("act")
                        if int8:
                            prog.op("act", "activation", (act, act.inc(1)),
                                    out=sgl[:], in_=ps_g[:, :], func=AF.Silu,
                                    scale=sg[:, hh:hh + 1])
                        else:
                            prog.op("act", "activation", (act, act.inc(1)),
                                    out=sgl[:], in_=ps_g[:, :], func=AF.Silu)
                        prog.wait("dve", act, act.n)
                        prog.settle("dve")
                        if (hq == NQ - 1 and hc == 3 and hs == 1
                                and tcI == NCO - 1):
                            for _ in range(8):
                                prog.settle("dve")
                        gcol = hloc * TLOC + tcI * TC
                        if int8:
                            prog.op("dve", "scalar_tensor_tensor", None,
                                    out=gstage[:], in0=ps_u[:, :],
                                    scalar=su[:, hh:hh + 1], in1=sgl[:],
                                    op0=ALU.mult, op1=ALU.mult)
                        else:
                            prog.op("dve", "tensor_mul", None,
                                    out=gstage[:], in0=ps_u[:, :],
                                    in1=sgl[:])
                        prog.op("dve", "tensor_copy", (dve, dve.inc(1)),
                                out=gated[:, gcol:gcol + TC], in_=gstage[:])
                gu_d[hc] = pe.n
            prog.mm(ps_jk[0:1, :], ones_col[:], sgl[:], True, True,
                    inc=(pe, pe.inc(1)))
            prog.fence("dve", dve)
            if int8 and KPHASE2 == 6 and hq == KQD:
                prog.wait("sp", dve, dve.n)
                prog.wait("pool", dve, dve.n)
                dsDm4 = DStream(prog, nc, ctx, 1)
                tokD4 = dsDm4.load("pool", 0, [
                    (final_dst_t[:, 0:NSUB, :],
                     gated[:].rearrange("p (n t) -> p n t", n=NSUB)),
                    (final_dst_t[:, 8, 0:TC], sgl[:]),
                    (final_dst_t[:, 9, 0:TC], gstage[:])])
                dsDm4.wait("sp", tokD4)
                dsDm4.wait("pool", tokD4)
                dsDm4.wait("dve", tokD4)
            prog.wait("pe", dve, dve.n)
            dpd, ded, std = {}, {}, {}
            for ot in range(NT):
                ws = ot % 2
                wsl = slice(ws * NSUB * 128, (ws + 1) * NSUB * 128)
                tokD = dsD.load("pool", ws, [
                    (wd_sb[:, wsl].rearrange("p (n c) -> p n c", n=NSUB),
                     wdT_t[:, NSUB * hq:NSUB * (hq + 1),
                           ot * 128:(ot + 1) * 128])],
                    gate=(pe, dpd[ot - 2]) if ot >= 2 else
                         (pe, pe.n))
                dsD.wait("pe", tokD)
                for tcI in range(NCO):
                    key = ot * NCO + tcI
                    osl = key % 2
                    if key >= (1 if KBE else 2):
                        prog.wait("pe", dve, ded[key - (1 if KBE else 2)])
                    for n in range(NSUB):
                        prog.mm(ps_d[:, osl * TC:(osl + 1) * TC],
                                wd_sb[:, ws * NSUB * 128 + n * 128:
                                      ws * NSUB * 128 + n * 128 + 128],
                                gated[:, n * TLOC + tcI * TC:
                                      n * TLOC + (tcI + 1) * TC],
                                n == 0, n == NSUB - 1,
                                inc=(pe, pe.inc(1)) if n == NSUB - 1 else None)
                    dpd[ot] = pe.n
                    prog.flush(ps_jk[0:1, 0:8])
                    prog.wait("dve", pe, pe.n)
                    prog.settle("dve")
                    if hq == 0 and not int8:
                        tokR = dsR.load("sp", osl, [
                            (rbuf[:, osl * TC:(osl + 1) * TC],
                             src_t[:, ot, tcI * TC:(tcI + 1) * TC])],
                            gate=(dve, ded[key - 2]) if key >= 2 else None)
                        dsR.wait("dve", tokR)
                        if key >= 2:
                            dsS.wait("dve", std[key - 2])
                        prog.op("dve", "tensor_add", (dve, dve.inc(1)),
                                out=ebuf[:, osl * TC:(osl + 1) * TC],
                                in0=ps_d[:, osl * TC:(osl + 1) * TC],
                                in1=rbuf[:, osl * TC:(osl + 1) * TC])
                    else:
                        if key >= 2:
                            dsS.wait("dve", std[key - 2])
                        prog.op("dve", "tensor_copy", (dve, dve.inc(1)),
                                out=ebuf[:, osl * TC:(osl + 1) * TC],
                                in_=ps_d[:, osl * TC:(osl + 1) * TC])
                    ded[key] = dve.n
                    prog.wait("pool", dve, ded[key])
                    std[key] = dsS.load("pool", osl, [
                        (dst_t[:, ot, tcI * TC:(tcI + 1) * TC],
                         ebuf[:, osl * TC:(osl + 1) * TC])], accum=(hq > 0))
            for kk in (NT * NCO - 2, NT * NCO - 1):
                dsS.wait("pool", std[kk])
            if int8 and KPHASE2 == 7 and hq == KQD:
                prog.wait("sp", dve, dve.n)
                prog.wait("pool", dve, dve.n)
                dsDm5 = DStream(prog, nc, ctx, 1)
                tokD5 = dsDm5.load("pool", 0, [
                    (final_dst_t[:, 0:NSUB, :],
                     gated[:].rearrange("p (n t) -> p n t", n=NSUB))])
                dsDm5.wait("sp", tokD5)
                dsDm5.wait("pool", tokD5)
                dsDm5.wait("dve", tokD5)
                dsDm5.wait("pe", tokD5)
        if int8 and KPHASE2 == 4:
            dsS.wait("sp", std[NT * NCO - 2])
            dsS.wait("sp", std[NT * NCO - 1])
            dsDm2 = DStream(prog, nc, ctx, 1)
            tokD2 = dsDm2.load("sp", 0, [
                (final_dst_t[:, :, :], dst_t[:, :, :])])
            dsDm2.wait("sp", tokD2)
        if int8 and KPHASE2 not in (4, 5, 6, 7):
            dsS.wait("sp", std[NT * NCO - 2])
            dsS.wait("sp", std[NT * NCO - 1])
            dsF = DStream(prog, nc, ctx, 2)
            dsO = DStream(prog, nc, ctx, 2)
            f_d, o_d = {}, {}
            for ot in range(NT):
                for tcI in range(NCO):
                    key = ot * NCO + tcI
                    osl = key % 2
                    tokF = dsF.load("sp", osl, [
                        (rbuf[:, osl * TC:(osl + 1) * TC],
                         dst_t[:, ot, tcI * TC:(tcI + 1) * TC]),
                        (xbuf[:, osl * TC:(osl + 1) * TC],
                         resid_t[:, ot, tcI * TC:(tcI + 1) * TC])],
                        gate=(dve, f_d[key - 2]) if key >= 2 else None)
                    dsF.wait("dve", tokF)
                    if key >= 2:
                        dsO.wait("dve", o_d[key - 2])
                    prog.op("dve", "scalar_tensor_tensor", (dve, dve.inc(1)),
                            out=ebuf[:, osl * TC:(osl + 1) * TC],
                            in0=rbuf[:, osl * TC:(osl + 1) * TC],
                            scalar=sd[:, ot:ot + 1],
                            in1=xbuf[:, osl * TC:(osl + 1) * TC],
                            op0=ALU.mult, op1=ALU.add)
                    f_d[key] = dve.n
                    prog.wait("sp", dve, dve.n)
                    o_d[key] = dsO.load("sp", osl, [
                        (final_dst_t[:, ot, tcI * TC:(tcI + 1) * TC],
                         ebuf[:, osl * TC:(osl + 1) * TC])])
            for kk in (NT * NCO - 2, NT * NCO - 1):
                dsO.wait("sp", o_d[kk])


def emit_l1kvr(prog, nc, ctx, eng, r2_t, l1kT, l1vT, l1rT, l1k_t, l1v_t,
               rsig_t, ones_col, ones_row):
    pe, act, dve = eng["pe"], eng["act"], eng["dve"]
    with ExitStack() as P:
        xbuf = sbuf(P, nc, "cxb", 128, NT * TC, F32)
        lnx3 = sbuf(P, nc, "cln", 128, NT * TLOC, BF16)
        w_sb = sbuf(P, nc, "cw", 128, 2 * NT * 128, BF16)
        ev = sbuf(P, nc, "cev", 128, 2 * TC, F32)
        evb = sbuf(P, nc, "cevb", 128, 2 * TC, BF16)
        G = ln_scratch(P, nc, "c", pe, act, dve, ones_col, ones_row)
        ps_p = psum(P, nc, "cpp", 128, 2 * TC)
        ps_jk = psum(P, nc, "cjk", 1, 8)
        dsX = DStream(prog, nc, ctx, 1)
        dsW = DStream(prog, nc, ctx, 2)
        dsS = DStream(prog, nc, ctx, 2)
        for cci in range(NCO):
            t0 = cci * TC
            tokX = dsX.load("sp", 0, [
                (xbuf[:].rearrange("p (n t) -> p n t", n=NT),
                 r2_t[:, :, t0:t0 + TC])],
                gate=(dve, dve.n) if cci > 0 else None)
            if cci > 0:
                prog.wait("sp", act, act.n)
                prog.wait("sp", pe, pe.n)
            dsX.wait("dve", tokX)
            dsX.wait("act", tokX)
            if cci > 0:
                prog.wait("dve", pe, pe.n)
                prog.wait("act", pe, pe.n)
            emit_ln(prog, G, xbuf, lnx3, t0, TLOC)
        prog.wait("pe", dve, dve.n)
        cpd, ced, cst = {}, {}, {}
        for wi, (wT, out_t, sig) in list(enumerate(
                ((l1kT, l1k_t, False), (l1vT, l1v_t, False),
                 (l1rT, rsig_t, True))))[:KC1W]:
            wT_t = wtiled(wT)
            for ct in range(NT):
                key = wi * NT + ct
                ws = key % 2
                wsl = slice(ws * NT * 128, (ws + 1) * NT * 128)
                tokW = dsW.load("pool", ws, [
                    (w_sb[:, wsl].rearrange("p (n c) -> p n c", n=NT),
                     wT_t[:, :, ct * 128:(ct + 1) * 128])],
                    gate=(pe, cpd[key - 2]) if key >= 2 else None)
                dsW.wait("pe", tokW)
                for tcI in range(NCO):
                    k2 = key * NCO + tcI
                    osl = k2 % 2
                    if k2 >= (1 if KBD else 2):
                        e, v = ced[k2 - (1 if KBD else 2)]
                        prog.wait("pe", e, v)
                    for n in range(NT):
                        prog.mm(ps_p[:, osl * TC:(osl + 1) * TC],
                                w_sb[:, ws * NT * 128 + n * 128:
                                     ws * NT * 128 + n * 128 + 128],
                                lnx3[:, n * TLOC + tcI * TC:
                                     n * TLOC + (tcI + 1) * TC],
                                n == 0, n == NT - 1,
                                inc=(pe, pe.inc(1)) if n == NT - 1 else None)
                    cpd[key] = pe.n
                    if sig:
                        prog.flush(ps_jk[0:1, :])
                        prog.wait("dve", pe, pe.n)
                        prog.wait("dve", act, act.n)
                        if k2 >= 2:
                            dsS.wait("dve", cst[k2 - 2])
                        prog.settle("dve")
                        prog.op("dve", "tensor_copy", (dve, dve.inc(1)),
                                out=ev[:, osl * TC:(osl + 1) * TC],
                                in_=ps_p[:, osl * TC:(osl + 1) * TC])
                        ced[k2] = (dve, dve.n)
                        prog.wait("act", dve, dve.n)
                        prog.op("act", "activation", (act, act.inc(1)),
                                out=evb[:, osl * TC:(osl + 1) * TC],
                                in_=ev[:, osl * TC:(osl + 1) * TC],
                                func=AF.Sigmoid)
                        prog.wait("pool", act, act.n)
                        cst[k2] = dsS.load("pool", osl, [
                            (out_t[:, ct, tcI * TC:(tcI + 1) * TC],
                             evb[:, osl * TC:(osl + 1) * TC])])
                    else:
                        prog.flush(ps_jk[0:1, :])
                        prog.wait("dve", pe, pe.n)
                        if k2 >= 2:
                            dsS.wait("dve", cst[k2 - 2])
                        prog.settle("dve")
                        prog.op("dve", "tensor_copy", (dve, dve.inc(1)),
                                out=ev[:, osl * TC:(osl + 1) * TC],
                                in_=ps_p[:, osl * TC:(osl + 1) * TC])
                        ced[k2] = (dve, dve.n)
                        prog.wait("pool", dve, dve.n)
                        cst[k2] = dsS.load("pool", osl, [
                            (out_t[:, ct, tcI * TC:(tcI + 1) * TC],
                             ev[:, osl * TC:(osl + 1) * TC])])
        for kk in (KC1W * NT * NCO - 2, KC1W * NT * NCO - 1):
            dsS.wait("pool", cst[kk])


def emit_finals(prog, nc, ctx, eng, l1k_t, l1v_t, tdT_sb, tfT_sb, isf_sb,
                sfin):
    pe, act, dve, pool = eng["pe"], eng["act"], eng["dve"], eng["pool"]
    with ExitStack() as P:
        kb = sbuf(P, nc, "skb", 128, 2 * TLOC, F32)
        vb = sbuf(P, nc, "svb", 128, 2 * TLOC, F32)
        w_ = sbuf(P, nc, "sw", 128, TLOC, F32)
        wv = sbuf(P, nc, "swv", 128, TLOC, F32)
        dpw = sbuf(P, nc, "sdpw", 128, TLOC, F32)
        junk = sbuf(P, nc, "sjunk", 128, TLOC, F32)
        iota_r = sbuf(P, nc, "siota", 128, TLOC, F32)
        ldc = sbuf(P, nc, "sldc", 128, NT, F32)
        fmc = sbuf(P, nc, "sfmc", 128, NT, F32)
        sf_sb = sbuf(P, nc, "ssf", 128, 2 * NT, F32)
        dsK = DStream(prog, nc, ctx, 2)
        dsS = DStream(prog, nc, ctx, 1)
        prog.op("pool", "iota", (pool, pool.inc(1)), out=iota_r[:],
                pattern=[[-1, TLOC]], base=TLOC - 1, channel_multiplier=0,
                allow_small_or_imprecise_dtypes=True)
        prog.wait("act", pool, pool.n)
        prog.op("act", "activation", (act, act.inc(1)), out=ldc[:],
                in_=tdT_sb[:], func=AF.Exp)
        prog.wait("dve", act, act.n)
        prog.op("dve", "tensor_scalar_mul", (dve, dve.inc(1)), out=ldc[:],
                in0=ldc[:], scalar1=-1.0)
        ld_ready = dve.n
        prog.op("act", "activation", (act, act.inc(1)), out=fmc[:],
                in_=tfT_sb[:], func=AF.Exp)
        prog.wait("dve", act, act.n)
        prog.op("dve", "scalar_tensor_tensor", (dve, dve.inc(1)), out=fmc[:],
                in0=fmc[:], scalar=isf_sb[:, 0:1],
                in1=isf_sb[:, 1:2].broadcast_to([128, NT]),
                op0=ALU.mult, op1=ALU.add)
        fm_ready = dve.n
        prog.wait("act", dve, fm_ready)
        tok_d = {}
        for n in range(NT):
            slot = n % 2
            ssl = slice(slot * TLOC, (slot + 1) * TLOC)
            tokK = dsK.load("sp", slot, [
                (kb[:, ssl], l1k_t[:, n, :]), (vb[:, ssl], l1v_t[:, n, :])],
                gate=(dve, tok_d[n - 2]) if n >= 2 else None)
            dsK.wait("dve", tokK)
            prog.op("dve", "tensor_scalar", (dve, dve.inc(1)), out=w_[:],
                    in0=kb[:, ssl], scalar1=10.0, scalar2=-10.0,
                    op0=ALU.min, op1=ALU.max)
            prog.wait("act", dve, dve.n)
            prog.op("act", "activation", (act, act.inc(1)), out=w_[:],
                    in_=w_[:], func=AF.Exp)
            prog.op("act", "activation", (act, act.inc(1)), out=dpw[:],
                    in_=iota_r[:], func=AF.Exp, scale=ldc[:, n:n + 1])
            prog.wait("dve", act, act.n)
            prog.op("dve", "tensor_mul", None, out=w_[:, 0:1],
                    in0=w_[:, 0:1], in1=fmc[:, n:n + 1])
            prog.op("dve", "tensor_mul", None, out=wv[:], in0=w_[:],
                    in1=vb[:, ssl])
            prog.op("dve", "scalar_tensor_tensor", None, out=junk[:],
                    in0=wv[:], scalar=1.0, in1=dpw[:], op0=ALU.mult,
                    op1=ALU.mult, accum_out=sf_sb[:, n:n + 1])
            prog.op("dve", "scalar_tensor_tensor", (dve, dve.inc(1)),
                    out=junk[:], in0=w_[:], scalar=1.0, in1=dpw[:],
                    op0=ALU.mult, op1=ALU.mult,
                    accum_out=sf_sb[:, NT + n:NT + n + 1])
            tok_d[n] = dve.n
        prog.wait("sp", dve, dve.n)
        tokS = dsS.load("sp", 0, [(sfin[:], sf_sb[:])])
        dsS.wait("sp", tokS)


def build_launch2():
    nc = bass.Bass("TRN2", target_bir_lowering=False,
                   detect_race_conditions=False)

    def din(name, shape, dt=F32):
        return nc.dram_tensor(name, list(shape), dt, kind="ExternalInput").ap()

    def dout(name, shape, dt=F32):
        return nc.dram_tensor(name, list(shape), dt, kind="ExternalOutput").ap()

    def dint(name, shape, dt=F32):
        kind = "ExternalOutput" if name in DBG else "Internal"
        return nc.dram_tensor(name, list(shape), dt, kind=kind).ap()

    resid2 = din("resid2", (D, TLOC))
    l1k = din("l1k", (D, TLOC))
    l1v = din("l1v", (D, TLOC))
    rsig = din("rsig", (D, TLOC), BF16)
    sprev = din("sprev", (4, 128, 2 * NT))
    pwv = din("pwv", (128, 4))
    tdT = din("tdT", (128, NT))
    tfT = din("tfT", (128, NT))
    isf = din("isf", (128, 2))
    l1oT = din("l1oT", (D, D))
    qg = din("qg", (D, FFH), I8)
    qu = din("qu", (D, FFH), I8)
    qd = din("qd", (FFH, D), I8)
    sgT = din("sgT", (128, FFH // 128))
    suT = din("suT", (128, FFH // 128))
    sdT = din("sdT", (128, NT))
    onesColIn = din("onesColIn", (128, 1), BF16)
    onesRowIn = din("onesRowIn", (1, 128), BF16)

    outT = dout("outT", (D, TLOC))
    resid3 = dint("resid3", (D, TLOC))
    dacc = dint("dacc", (D, TLOC))

    r2_t, l1k_t, l1v_t = tiled(resid2), tiled(l1k), tiled(l1v)
    rsig_t, out_t = tiled(rsig), tiled(outT)
    r3_t, dacc_t = tiled(resid3), tiled(dacc)

    prog = Prog()
    with ExitStack() as ctx:
        sems = {}
        for nm in ("pe", "act", "dve", "pool", "spq"):
            sems[nm] = S(ctx.enter_context(nc.semaphore(nm)), nm)
        pe, act, dve = sems["pe"], sems["act"], sems["dve"]
        eng = {"pe": pe, "act": act, "dve": dve, "pool": sems["pool"],
               "sp": sems["spq"]}
        jkd = sbuf(ctx, nc, "jkd", 128, 1, F32)
        jkdo = sbuf(ctx, nc, "jkdo", 128, 1, F32)
        jka = sbuf(ctx, nc, "jka", 128, 1, F32)
        jkao = sbuf(ctx, nc, "jkao", 128, 512, F32)
        prog.junk = {
            "dve": lambda E: E.reduce_sum(
                out=jkdo[:], in_=jkd[:, 0:1].broadcast_to([128, 512]),
                axis=AX.X),
            "act": lambda E: E.activation(
                out=jkao[:], in_=jka[:, 0:1].broadcast_to([128, 512]),
                func=AF.Copy),
        }
        prog.emit("dve", lambda E: E.memzero(jkd[:]))
        prog.emit("act", lambda E: E.memzero(jka[:]))
        jmv = sbuf(ctx, nc, "jmv", 128, 8, BF16)
        prog.emit("dve", lambda E: E.memzero(jmv[:]))

        def _flush(junk):
            prog.mm(junk, ones_col[:], jmv[:], True, True,
                    inc=(pe, pe.inc(1)))
        prog.flush = _flush
        ones_col = sbuf(ctx, nc, "ones_col", 128, 1, BF16)
        ones_row = sbuf(ctx, nc, "ones_row", 1, 128, BF16)
        tdT_sb = sbuf(ctx, nc, "tdT_sb", 128, NT, F32)
        tfT_sb = sbuf(ctx, nc, "tfT_sb", 128, NT, F32)
        isf_sb = sbuf(ctx, nc, "isf_sb", 128, 2, F32)
        sg_sb = sbuf(ctx, nc, "sg_sb", 128, FFH // 128, F32)
        su_sb = sbuf(ctx, nc, "su_sb", 128, FFH // 128, F32)
        sd_sb = sbuf(ctx, nc, "sd_sb", 128, NT, F32)
        sprev_sb = sbuf(ctx, nc, "sprev_sb", 128, 4 * 2 * NT, F32)
        pw_sb = sbuf(ctx, nc, "pw_sb", 128, 4, F32)

        dsC = DStream(prog, nc, ctx, 1)
        tok = dsC.load("sp", 0, [
            (ones_col[:], onesColIn[:]), (ones_row[:], onesRowIn[:]),
            (tdT_sb[:], tdT[:]), (tfT_sb[:], tfT[:]), (isf_sb[:], isf[:]),
            (sg_sb[:], sgT[:]), (su_sb[:], suT[:]), (sd_sb[:], sdT[:]),
            (sprev_sb[:].rearrange("p (j c) -> p j c", j=4),
             sprev.rearrange("j p c -> p j c")),
            (pw_sb[:], pwv[:])])
        for e in ("pe", "act", "dve", "pool"):
            dsC.wait(e, tok)

        with ExitStack() as P:
            kb = sbuf(P, nc, "kb", 128, 2 * TLOC, F32)
            vb = sbuf(P, nc, "vb", 128, 2 * TLOC, F32)
            rs = sbuf(P, nc, "rs", 128, 2 * TLOC, BF16)
            w_ = sbuf(P, nc, "w_", 128, TLOC, F32)
            wv = sbuf(P, nc, "wv", 128, TLOC, F32)
            num = sbuf(P, nc, "num", 128, TLOC, F32)
            den = sbuf(P, nc, "den", 128, TLOC, F32)
            ldc = sbuf(P, nc, "ldc", 128, NT, F32)
            dk = sbuf(P, nc, "dk", 128, NT, F32)
            fmc = sbuf(P, nc, "fmc", 128, NT, F32)
            wj = sbuf(P, nc, "wj", 128, NT, F32)
            ej = sbuf(P, nc, "ej", 128, NT, F32)
            tn = sbuf(P, nc, "tn", 128, NT, F32)
            td = sbuf(P, nc, "td", 128, NT, F32)
            rwkv = sbuf(P, nc, "rwkv", 128, NT * TLOC, BF16)
            wo_sb = sbuf(P, nc, "wo_sb", 128, 2 * NT * 128, BF16)
            rbuf = sbuf(P, nc, "rbuf", 128, 2 * TC, F32)
            ebuf = sbuf(P, nc, "ebuf", 128, 2 * TC, F32)
            ps_p = psum(P, nc, "dps", 128, 2 * TC)
            ps_jk = psum(P, nc, "djk", 1, 8)
            dsK = DStream(prog, nc, ctx, 2)
            dsW = DStream(prog, nc, ctx, 2)
            dsR = DStream(prog, nc, ctx, 2)
            dsS = DStream(prog, nc, ctx, 2)
            prog.op("act", "activation", (act, act.inc(1)), out=ldc[:],
                    in_=tdT_sb[:], func=AF.Exp)
            prog.wait("dve", act, act.n)
            prog.settle("dve")
            prog.op("dve", "tensor_scalar_mul", (dve, dve.inc(1)), out=ldc[:],
                    in0=ldc[:], scalar1=-1.0)
            prog.wait("act", dve, dve.n)
            prog.settle("act")
            prog.op("act", "activation", (act, act.inc(1)), out=dk[:],
                    in_=ldc[:], func=AF.Exp)
            prog.op("act", "activation", (act, act.inc(1)), out=fmc[:],
                    in_=tfT_sb[:], func=AF.Exp)
            prog.wait("dve", act, act.n)
            prog.settle("dve")
            prog.op("dve", "scalar_tensor_tensor", (dve, dve.inc(1)),
                    out=fmc[:], in0=fmc[:], scalar=isf_sb[:, 0:1],
                    in1=isf_sb[:, 1:2].broadcast_to([128, NT]),
                    op0=ALU.mult, op1=ALU.add)
            prog.op("dve", "memset", None, ap=tn[:], constant=0.0)
            prog.op("dve", "memset", (dve, dve.inc(1)), ap=td[:], constant=0.0)
            for j in range(4):
                prog.op("dve", "tensor_scalar", (dve, dve.inc(1)), out=ej[:],
                        in0=ldc[:], scalar1=float(TLOC), scalar2=None,
                        op0=ALU.mult, op1=ALU.bypass)
                prog.settle("dve")
                prog.op("dve", "tensor_scalar_mul", (dve, dve.inc(1)),
                        out=ej[:], in0=ej[:], scalar1=pw_sb[:, j:j + 1])
                prog.wait("act", dve, dve.n)
                prog.settle("act")
                prog.op("act", "activation", (act, act.inc(1)), out=wj[:],
                        in_=ej[:], func=AF.Exp)
                prog.wait("dve", act, act.n)
                prog.settle("dve")
                prog.op("dve", "tensor_mul", None, out=ej[:], in0=wj[:],
                        in1=sprev_sb[:, j * 2 * NT:j * 2 * NT + NT])
                prog.settle("dve")
                prog.op("dve", "tensor_add", None, out=tn[:], in0=tn[:],
                        in1=ej[:])
                prog.settle("dve")
                prog.op("dve", "tensor_mul", None, out=ej[:], in0=wj[:],
                        in1=sprev_sb[:, j * 2 * NT + NT:(j + 1) * 2 * NT])
                prog.settle("dve")
                prog.op("dve", "tensor_add", (dve, dve.inc(1)), out=td[:],
                        in0=td[:], in1=ej[:])
                prog.settle("dve")
            init_ready = dve.n
            tok_d = {}
            for n in range(NT):
                slot = n % 2
                ssl = slice(slot * TLOC, (slot + 1) * TLOC)
                tokK = dsK.load("sp", slot, [
                    (kb[:, ssl], l1k_t[:, n, :]), (vb[:, ssl], l1v_t[:, n, :]),
                    (rs[:, ssl], rsig_t[:, n, :])],
                    gate=(dve, tok_d[n - 2]) if n >= 2 else None)
                dsK.wait("dve", tokK)
                prog.op("dve", "tensor_scalar", (dve, dve.inc(1)), out=w_[:],
                        in0=kb[:, ssl], scalar1=10.0, scalar2=-10.0,
                        op0=ALU.min, op1=ALU.max)
                prog.wait("act", dve, dve.n)
                prog.op("act", "activation", (act, act.inc(1)), out=w_[:],
                        in_=w_[:], func=AF.Exp)
                prog.wait("dve", act, act.n)
                prog.settle("dve")
                prog.op("dve", "tensor_mul", None, out=w_[:, 0:1],
                        in0=w_[:, 0:1], in1=fmc[:, n:n + 1])
                prog.settle("dve")
                prog.op("dve", "tensor_mul", None, out=wv[:], in0=w_[:],
                        in1=vb[:, ssl])
                d0 = dk[:, n:n + 1].broadcast_to([128, TLOC])
                prog.op("dve", "tensor_tensor_scan", None, out=num[:],
                        data0=d0, data1=wv[:], initial=tn[:, n:n + 1],
                        op0=ALU.mult, op1=ALU.add)
                prog.op("dve", "tensor_tensor_scan", None, out=den[:],
                        data0=d0, data1=w_[:], initial=td[:, n:n + 1],
                        op0=ALU.mult, op1=ALU.add)
                prog.op("dve", "tensor_scalar_add", None, out=den[:],
                        in0=den[:], scalar1=1e-6)
                prog.op("dve", "reciprocal", None, out=den[:], in_=den[:])
                prog.op("dve", "tensor_mul", None, out=num[:], in0=num[:],
                        in1=den[:])
                prog.op("dve", "tensor_mul", (dve, dve.inc(1)),
                        out=rwkv[:, n * TLOC:(n + 1) * TLOC], in0=num[:],
                        in1=rs[:, ssl])
                tok_d[n] = dve.n
            rwkv_ready = dve.n
            if KPHASE2 == 1:
                prog.wait("sp", dve, rwkv_ready)
                prog.wait("pool", dve, rwkv_ready)
                dsDmp = DStream(prog, nc, ctx, 1)
                if KL2DBG:
                    tokDm = dsDmp.load("pool", 0, [
                        (out_t[:, 0, :], w_[:]),
                        (out_t[:, 1, :], den[:]),
                        (out_t[:, 2, :], num[:]),
                        (out_t[:, 3, :], kb[:, TLOC:2 * TLOC]),
                        (out_t[:, 4, :], vb[:, TLOC:2 * TLOC]),
                        (out_t[:, 5, :], wv[:]),
                        (out_t[:, 6, 0:NT], tn[:]),
                        (out_t[:, 7, 0:NT], td[:]),
                        (out_t[:, 8, 0:NT], dk[:]),
                        (out_t[:, 9, 0:NT], fmc[:]),
                        (out_t[:, 10, 0:NT], ldc[:]),
                        (out_t[:, 11, :],
                         rwkv[:, 15 * TLOC:16 * TLOC]),
                        (out_t[:, 12, 0:128], sprev_sb[:]),
                        (out_t[:, 12, 128:132], pw_sb[:]),
                        (out_t[:, 12, 132:134], isf_sb[:])])
                else:
                    tokDm = dsDmp.load("pool", 0, [
                        (out_t[:, :, :],
                         rwkv[:].rearrange("p (n t) -> p n t", n=NT))])
                dsDmp.wait("sp", tokDm)
                dsDmp.wait("pool", tokDm)
            prog.wait("pe", dve, rwkv_ready)
            wo_t = wtiled(l1oT)
            opd, oed, std = {}, {}, {}
            for ot in range(NT) if KPHASE2 >= 2 else []:
                osl = ot % 2
                wsl = slice(osl * NT * 128, (osl + 1) * NT * 128)
                tokW = dsW.load("pool", osl, [
                    (wo_sb[:, wsl].rearrange("p (n c) -> p n c", n=NT),
                     wo_t[:, :, ot * 128:(ot + 1) * 128])],
                    gate=(pe, opd[ot - 2]) if ot >= 2 else None)
                dsW.wait("pe", tokW)
                for tcI in range(NCO):
                    key = ot * NCO + tcI
                    ksl = key % 2
                    if key >= 1:
                        prog.wait("pe", dve, oed[key - 1])
                    for n in range(NT):
                        prog.mm(ps_p[:, ksl * TC:(ksl + 1) * TC],
                                wo_sb[:, osl * NT * 128 + n * 128:
                                      osl * NT * 128 + n * 128 + 128],
                                rwkv[:, n * TLOC + tcI * TC:
                                     n * TLOC + (tcI + 1) * TC],
                                n == 0, n == NT - 1,
                                inc=(pe, pe.inc(1)) if n == NT - 1 else None)
                    opd[ot] = pe.n
                    prog.flush(ps_jk[0:1, :])
                    prog.wait("dve", pe, pe.n)
                    tokR = dsR.load("sp", ksl, [
                        (rbuf[:, ksl * TC:(ksl + 1) * TC],
                         r2_t[:, ot, tcI * TC:(tcI + 1) * TC])],
                        gate=(dve, oed[key - 2]) if key >= 2 else None)
                    dsR.wait("dve", tokR)
                    if key >= 2:
                        dsS.wait("dve", std[key - 2])
                    prog.settle("dve")
                    prog.op("dve", "tensor_add", (dve, dve.inc(1)),
                            out=ebuf[:, ksl * TC:(ksl + 1) * TC],
                            in0=ps_p[:, ksl * TC:(ksl + 1) * TC],
                            in1=rbuf[:, ksl * TC:(ksl + 1) * TC])
                    oed[key] = dve.n
                    prog.wait("sp", dve, dve.n)
                    std[key] = dsS.load("sp", ksl, [
                        ((out_t if KPHASE2 == 2 else r3_t)
                         [:, ot, tcI * TC:(tcI + 1) * TC],
                         ebuf[:, ksl * TC:(ksl + 1) * TC])])
            for kk in ((NT * NCO - 2, NT * NCO - 1) if KPHASE2 >= 2 else ()):
                dsS.wait("sp", std[kk])
        prog.barrier(eng)

        if KPHASE2 >= 3:
            emit_ffn(prog, nc, ctx, eng, src_t=r3_t, dst_t=dacc_t,
                 wgT_t=wtiled(qg), wuT_t=wtiled(qu), wdT_t=wtiled(qd),
                 scales=(sg_sb, su_sb, sd_sb), ones_col=ones_col,
                 ones_row=ones_row, pfx="q", resid_t=r3_t, final_dst_t=out_t)
        prog.barrier(eng)
        prog.run(nc)
    return nc


_NC_CACHE = {}


def _get_ncs():
    if "l1" not in _NC_CACHE:
        _NC_CACHE["l1"] = build_launch1()
        _NC_CACHE["l2"] = build_launch2()
    return _NC_CACHE["l1"], _NC_CACHE["l2"]


def _stage_core(c, inputs):
    b, q = c // 4, c % 4
    x = inputs["L0_attn_x"]
    s0 = q * TLOC
    xT = np.zeros((D, TEXT), np.float32)
    lo = max(0, s0 - HALO)
    xT[:, HALO - (s0 - lo):HALO] = np.asarray(x[b, lo:s0]).T
    xT[:, HALO:] = np.asarray(x[b, s0:s0 + TLOC]).T
    pos = np.arange(s0 - HALO, s0 + TLOC, dtype=np.float64)
    inv = 1.0 / (10000.0 ** (np.arange(0, HD, 2, dtype=np.float64) / HD))
    ang = pos[None, :] * inv[:, None]
    cosT = np.concatenate([np.cos(ang), np.cos(ang)], 0).astype(np.float32)
    sinT = np.concatenate([np.sin(ang), np.sin(ang)], 0).astype(np.float32)
    perm = np.zeros((128, 128), np.float32)
    for m in range(128):
        perm[(m + 64) % 128, m] = 1.0
    masks = np.zeros((16, 128, KSPAN), np.float32)
    for g in range(16):
        qabs = s0 + g * 128 + np.arange(128)
        kabs = s0 - HALO + g * 128 + np.arange(KSPAN)
        m = ((kabs[None, :] <= qabs[:, None])
             & (qabs[:, None] - kabs[None, :] < WIN)
             & (kabs[None, :] >= 0))
        masks[g] = m.astype(np.float32)
    isfv = np.zeros((128, 2), np.float32)
    isfv[:, 0] = 1.0 if q == 0 else 0.0
    isfv[:, 1] = 1.0 - isfv[:, 0]
    d = {
        "xT": xT,
        "wqT": np.ascontiguousarray(np.asarray(inputs["L0_attn_wq"]).T),
        "wkT": np.ascontiguousarray(np.asarray(inputs["L0_attn_wk"]).T),
        "wvT": np.ascontiguousarray(np.asarray(inputs["L0_attn_wv"]).T),
        "woT": np.ascontiguousarray(np.asarray(inputs["L0_attn_wo"]).T),
        "wgT": np.ascontiguousarray(np.asarray(inputs["L0_ffn_w_gate"]).T),
        "wuT": np.ascontiguousarray(np.asarray(inputs["L0_ffn_w_up"]).T),
        "wdT": np.ascontiguousarray(np.asarray(inputs["L0_ffn_w_down"]).T),
        "l1kT": np.ascontiguousarray(np.asarray(inputs["L1_attn_Wk"]).T),
        "l1vT": np.ascontiguousarray(np.asarray(inputs["L1_attn_Wv"]).T),
        "l1rT": np.ascontiguousarray(np.asarray(inputs["L1_attn_Wr"]).T),
        "cosT": cosT, "sinT": sinT,
        "qnw": np.asarray(inputs["L0_attn_q_norm_w"],
                          np.float32).reshape(128, 1),
        "knw": np.asarray(inputs["L0_attn_k_norm_w"],
                          np.float32).reshape(128, 1),
        "masksIn": masks.astype(BF16NP),
        "tdT": np.ascontiguousarray(
            np.asarray(inputs["L1_attn_time_decay"], np.float32).T),
        "tfT": np.ascontiguousarray(
            np.asarray(inputs["L1_attn_time_first"], np.float32).T),
        "isf": isfv,
        "identIn": np.eye(128, dtype=BF16NP),
        "permIn": perm.astype(BF16NP),
        "onesColIn": np.ones((128, 1), BF16NP),
        "onesRowIn": np.ones((1, 128), BF16NP),
    }
    return d


def _stage_core2(c, inputs, res1):
    b, q = c // 4, c % 4
    sp = np.zeros((4, 128, 2 * NT), np.float32)
    pw = np.full((128, 4), 1e9, np.float32)
    for j in range(q):
        sp[j] = res1[4 * b + (q - 1 - j)]["sfin"]
        pw[:, j] = float(j)
    isfv = np.zeros((128, 2), np.float32)
    isfv[:, 0] = 1.0 if q == 0 else 0.0
    isfv[:, 1] = 1.0 - isfv[:, 0]
    me = res1[c]
    return {
        "resid2": np.asarray(me["resid2"]),
        "l1k": np.asarray(me["l1k"]),
        "l1v": np.asarray(me["l1v"]),
        "rsig": np.asarray(me["rsig"]),
        "sprev": sp, "pwv": pw,
        "tdT": np.ascontiguousarray(
            np.asarray(inputs["L1_attn_time_decay"], np.float32).T),
        "tfT": np.ascontiguousarray(
            np.asarray(inputs["L1_attn_time_first"], np.float32).T),
        "isf": isfv,
        "l1oT": np.ascontiguousarray(np.asarray(inputs["L1_attn_Wo"]).T),
        "qg": np.asarray(inputs["L1_ffn_w_gate"]),
        "qu": np.asarray(inputs["L1_ffn_w_up"]),
        "qd": np.asarray(inputs["L1_ffn_w_down"]),
        "sgT": np.ascontiguousarray(
            np.asarray(inputs["L1_ffn_s_gate"], np.float32).reshape(64, 128).T),
        "suT": np.ascontiguousarray(
            np.asarray(inputs["L1_ffn_s_up"], np.float32).reshape(64, 128).T),
        "sdT": np.ascontiguousarray(
            np.asarray(inputs["L1_ffn_s_down"], np.float32).reshape(16, 128).T),
        "onesColIn": np.ones((128, 1), BF16NP),
        "onesRowIn": np.ones((1, 128), BF16NP),
    }


def kernel(**inputs):
    nc1, nc2 = _get_ncs()
    in1 = [_stage_core(c, inputs) for c in range(NCORES)]
    res1 = run_bass_kernel_spmd(nc1, in1, core_ids=list(range(NCORES))).results
    in2 = [_stage_core2(c, inputs, res1) for c in range(NCORES)]
    res2 = run_bass_kernel_spmd(nc2, in2, core_ids=list(range(NCORES))).results
    out = np.empty((2, 8192, D), np.float32)
    for c in range(NCORES):
        b, q = c // 4, c % 4
        out[b, q * TLOC:(q + 1) * TLOC] = np.asarray(res2[c]["outT"]).T
    return out


if __name__ == "__main__":
    n1 = build_launch1()
    print("launch1:", n1.n_instructions(), "instructions")
    n2 = build_launch2()
    print("launch2:", n2.n_instructions(), "instructions")
